# version 01 (final)
# baseline (speedup 1.0000x reference)
import numpy as np
import ml_dtypes
import concourse.bass as bass
import concourse.mybir as mybir
from concourse.bass_utils import run_bass_kernel_spmd

F32 = mybir.dt.float32
BF16 = mybir.dt.bfloat16
AF = mybir.ActivationFunctionType
ALU = mybir.AluOpType
AX = mybir.AxisListType
ts = bass.ts

B, T, C = 2, 8192, 1024
NCORE = 8
BLK = 1024
TPC = 2 * BLK
NTT = TPC // 128
NCC = C // 128
CAP = 2048
E0, H0 = 8, 4096
E1, I1, SHI = 32, 512, 1024
NHEAD, HD = 16, 64
SM_BOUND = 30.0
BISECT_ITERS = 48

bfd = ml_dtypes.bfloat16


def blocks_of(r):
    return (r % 4, 7 - (r % 4))


def core_token_rows(r):
    b = r // 4
    j0, j1 = blocks_of(r)
    return b, [slice(b * T + j0 * BLK, b * T + (j0 + 1) * BLK),
               slice(b * T + j1 * BLK, b * T + (j1 + 1) * BLK)]


def shard_tokens(xfull):
    out = []
    for r in range(NCORE):
        _, sl = core_token_rows(r)
        out.append(np.ascontiguousarray(
            np.concatenate([xfull[sl[0]], xfull[sl[1]]], 0)))
    return out


def unshard_tokens(per_core):
    full = np.empty((B * T,) + per_core[0].shape[1:], per_core[0].dtype)
    for r in range(NCORE):
        _, sl = core_token_rows(r)
        full[sl[0]] = per_core[r][:BLK]
        full[sl[1]] = per_core[r][BLK:]
    return full


def make_consts():
    c = {}
    c["ident_f32"] = np.eye(128, dtype=np.float32)
    c["ident_bf16"] = np.eye(128, dtype=bfd)
    G = np.zeros((128, 16), np.float32)
    for p in range(128):
        G[p, p // 8] = 1.0
    c["G"] = G
    c["GT"] = np.ascontiguousarray(G.T)
    GPRE = np.zeros((128, 128), np.float32)
    for p in range(128):
        for q in range(128):
            if p // 8 == q // 8 and (p % 8) < (q % 8):
                GPRE[p, q] = 1.0
    c["GPRE"] = GPRE
    c["ones_1x128_f32"] = np.ones((1, 128), np.float32)
    c["zeros_col"] = np.zeros((128, 1), np.float32)
    c["ones_col"] = np.ones((128, 1), np.float32)
    c["eps_col"] = np.full((128, 1), 1e-5, np.float32)
    c["ones_1024"] = np.ones((128, 1024), np.float32)
    c["ones_tile"] = np.ones((128, 512), np.float32)
    c["iota_row"] = np.arange(BLK, dtype=np.float32)[None, :]
    for d in (0, 128, 256, 384):
        m = np.zeros((128, 512), np.float32)
        for kk in range(128):
            m[kk, :max(0, min(512, d + kk))] = -1e30
        c[f"mask{d}"] = m
    return c


CONSTS = make_consts()


class Prog:
    def __init__(self, name):
        self.nc = bass.Bass("TRN2", target_bir_lowering=False, num_devices=8)
        self.name = name
        self._i = 0
        self.steps = []
        self.in_names = []

    def sem(self, name=None):
        self._i += 1
        return self.nc.alloc_semaphore(name or f"s{self._i}")

    def dram_in(self, name, shape, dtype=F32):
        self.in_names.append(name)
        return self.nc.dram_tensor(name, list(shape), dtype, kind="ExternalInput").ap()

    def dram_out(self, name, shape, dtype=F32):
        return self.nc.dram_tensor(name, list(shape), dtype, kind="ExternalOutput").ap()

    def dram_tmp(self, name, shape, dtype=F32):
        return self.nc.dram_tensor(name, list(shape), dtype, kind="Internal").ap()

    def sb(self, name, shape, dtype=F32):
        shape = list(shape)
        esz = 2 if dtype == BF16 else 4
        ncol = shape[-1]
        pad_cols = (-(ncol * esz) % 512) // esz
        full = shape[:-1] + [ncol + pad_cols]
        ap = self.nc.alloc_sbuf_tensor(name, full, dtype).ap()
        if pad_cols:
            ap = ap[tuple([slice(None)] * (len(shape) - 1) + [slice(0, ncol)])]
        return ap

    def sb_at(self, name, shape, dtype, offset):
        return self.nc.alloc_sbuf_tensor_at(name, list(shape), dtype,
                                            offset=offset).ap()

    def ps(self, name, shape, dtype=F32):
        return self.nc.alloc_psum_tensor(name, list(shape), dtype).ap()

    def step(self, engine, fn, pad=False):
        self.steps.append((engine, fn, pad))

    def build(self):
        nc = self.nc
        chain = self.sem("chain")
        dsems = {}
        dcnt = {}

        def dma(eng, out, in_, **kw):
            key = str(eng.engine)
            eng.dma_start(out=out, in_=in_, **kw).then_inc(dsems[key], 16)
            dcnt[key] += 16
        self.dma = dma

        pad_tile = self.sb("risk_pad", (128, 8), F32)

        from collections import defaultdict
        per_engine = defaultdict(list)
        for idx, (en, fn, pad) in enumerate(self.steps):
            per_engine[en].append((idx, fn, pad))

        with nc.Block() as blk:
            for en, lst in per_engine.items():
                def make_section(lst=lst, en=en):
                    def section(eng):
                        key = str(eng.engine)
                        dsems[key] = self.sem(f"dsem_{en}")
                        dcnt[key] = 0
                        for idx, fn, pad in lst:
                            if idx > 0:
                                eng.wait_ge(chain, idx)
                            before = dcnt.get(key, 0)
                            last = fn(eng)
                            if dcnt[key] > before:
                                assert last is None, "step mixed dma and engine op"
                                eng.wait_ge(dsems[key], dcnt[key])
                                last = eng.nop()
                            elif pad:
                                if en == "scalar":
                                    eng.copy(pad_tile[:, 0:1], pad_tile[:, 1:2])
                                    last = eng.copy(pad_tile[:, 0:1], pad_tile[:, 1:2])
                                else:
                                    eng.tensor_copy(pad_tile[:, 0:1], pad_tile[:, 1:2])
                                    last = eng.tensor_copy(pad_tile[:, 0:1],
                                                           pad_tile[:, 1:2])
                            last.then_inc(chain, 1)
                    return section
                getattr(blk, en)(make_section())
        nc.finalize()
        return self


def run_prog(p, in_maps, tag=""):
    res = run_bass_kernel_spmd(p.nc, in_maps, core_ids=list(range(NCORE)))
    return res.results


def const_inmap(names):
    return {("c_" + n): CONSTS[n] for n in names}


def setup_consts(p, names):
    aps = {}
    drs = {}
    for n in names:
        arr = CONSTS[n]
        dt = BF16 if arr.dtype == bfd else F32
        drs[n] = p.dram_in("c_" + n, arr.shape, dt)
        aps[n] = p.sb("sb_" + n, arr.shape, dt)
    p.cdram = drs

    def fn(eng, names=tuple(names)):
        for n in names:
            p.dma(eng, aps[n][:], drs[n])
    p.step("sync", fn)
    return aps


class LNState:

    def __init__(self, p):
        self.xt = p.sb("ln_xt", (128, 1024), F32)
        self.s1 = p.sb("ln_s1", (128, 1), F32)
        self.junk = p.sb("ln_junk", (128, 1024), F32)
        self.s2 = p.sb("ln_s2", (128, 1), F32)
        self.m = p.sb("ln_m", (128, 1), F32)
        self.var = p.sb("ln_var", (128, 1), F32)
        self.sq = p.sb("ln_sq", (128, 1), F32)
        self.lnh = p.sb("ln_lnh", (128, 1), F32)
        self.rstd = p.sb("ln_rstd", (128, 1), F32)
        self.xn = p.sb("ln_xn", (128, 1024), BF16)
        self.pst = p.ps("ln_pst", (128, 128), BF16)
        self.eps = None
        self.zero = None

    def bind_consts(self, consts):
        self.eps = consts["eps_col"]
        self.zero = consts["zeros_col"]


def emit_ln_tile(p, st, x_dram, i, xnt, consts, xn_tm=None):
    def load(eng, i=i):
        p.dma(eng, st.xt[:], x_dram[128 * i:128 * (i + 1), :])
    p.step("sync", load)

    p.step("vector", lambda v: v.tensor_reduce(st.s1[:], st.xt[:], axis=AX.X,
                                               op=ALU.add), pad=True)
    p.step("gpsimd", lambda g: g.tensor_scalar(st.m[:], st.s1[:], 1.0 / 1024.0,
                                               0.0, ALU.mult, ALU.add))

    def stats2(v):
        v.tensor_scalar(st.junk[:], st.xt[:], st.m[:], 1.0, ALU.subtract, ALU.mult)
        return v.tensor_tensor(st.junk[:], st.junk[:], st.junk[:], op=ALU.mult)
    p.step("vector", stats2)
    p.step("vector", lambda v: v.tensor_reduce(st.s2[:], st.junk[:], axis=AX.X,
                                               op=ALU.add), pad=True)
    p.step("gpsimd", lambda g: g.tensor_scalar(st.var[:], st.s2[:], 1.0 / 1024.0,
                                               0.0, ALU.mult, ALU.add))
    p.step("scalar", lambda sc: sc.activation(st.sq[:], st.var[:], AF.Ln,
                                              bias=st.eps[:], scale=1.0))
    p.step("gpsimd", lambda g: g.tensor_scalar(st.lnh[:], st.sq[:], -0.5, 0.0,
                                               ALU.mult, ALU.add))
    p.step("scalar", lambda sc: sc.activation(st.rstd[:], st.lnh[:], AF.Exp,
                                              bias=st.zero[:], scale=1.0))

    def apply(v):
        last = v.tensor_scalar(st.xn[:], st.xt[:], st.m[:], st.rstd[:],
                               ALU.subtract, ALU.mult)
        if xn_tm is not None:
            last = v.tensor_copy(xn_tm[:, 1024 * i:1024 * (i + 1)], st.xn[:])
        return last
    p.step("vector", apply)
    for cc in range(NCC):
        p.step("tensor", lambda t, cc=cc: t.transpose(
            st.pst[:], st.xn[:, 128 * cc:128 * (cc + 1)], consts["ident_bf16"][:]))
        p.step("vector", lambda v, cc=cc, i=i: v.tensor_copy(
            xnt[cc][:, 128 * i:128 * (i + 1)], st.pst[:]))


def emit_ln_all(p, st, x_dram, xnt, consts, xn_tm=None):
    for i in range(NTT):
        emit_ln_tile(p, st, x_dram, i, xnt, consts, xn_tm)


def emit_gate_probs(p, xnt, gw_sb, probsT, consts):
    psc = p.ps("gp_psc", (128, E0), F32)
    pst = p.ps("gp_pst", (E0, 128), F32)
    sc_sb = p.sb("gp_sc", (128, E0), F32)
    mx = p.sb("gp_mx", (128, 1), F32)
    nmx = p.sb("gp_nmx", (128, 1), F32)
    sume = p.sb("gp_sum", (128, 1), F32)
    rsum = p.sb("gp_rsum", (128, 1), F32)
    prob = p.sb("gp_p", (128, E0), F32)

    for i in range(NTT):
        def mm(t, i=i):
            for cc in range(NCC):
                last = t.matmul(psc[:], xnt[cc][:, 128 * i:128 * (i + 1)],
                                gw_sb[cc][:], start=(cc == 0), stop=(cc == NCC - 1))
            return last
        p.step("tensor", mm)

        p.step("vector", lambda v: v.tensor_copy(sc_sb[:], psc[:]))
        p.step("vector", lambda v: v.reduce_max(mx[:], sc_sb[:], axis=AX.X),
               pad=True)
        p.step("gpsimd", lambda g: g.tensor_scalar(nmx[:], mx[:], -1.0, 0.0,
                                                   ALU.mult, ALU.add))
        p.step("scalar", lambda sc: sc.activation(
            prob[:], sc_sb[:], AF.Exp, bias=nmx[:], scale=1.0, accum_out=sume[:]), pad=True)
        p.step("vector", lambda v: v.reciprocal(rsum[:], sume[:]), pad=True)
        p.step("gpsimd", lambda g: g.tensor_scalar(prob[:], prob[:], rsum[:], 1.0,
                                                   ALU.mult, ALU.mult))
        p.step("tensor", lambda t: t.transpose(pst[:], prob[:], consts["ident_f32"][:]))
        p.step("vector", lambda v, i=i: v.tensor_copy(
            probsT[:, 128 * i:128 * (i + 1)], pst[:]))


_CONST_A = ["ident_bf16", "ident_f32", "eps_col", "zeros_col"]


def build_launch_A(debug=False):
    p = Prog("A")
    x = p.dram_in("x", (TPC, C))
    gw = p.dram_in("gate_w", (C, E0))
    probs_out = p.dram_out("probsT", (E0, TPC))

    consts = setup_consts(p, _CONST_A)
    xnt = [p.sb(f"xnt{cc}", (128, TPC), BF16) for cc in range(NCC)]
    gw_sb = [p.sb(f"gw{cc}", (128, E0), BF16) for cc in range(NCC)]
    probsT = p.sb("probsT_sb", (E0, TPC), F32)
    st = LNState(p)
    st.bind_consts(consts)

    def load_gw(eng):
        for cc in range(NCC):
            p.dma(eng, gw_sb[cc][:], gw[128 * cc:128 * (cc + 1), :])
    p.step("gpsimd", load_gw)

    emit_ln_all(p, st, x, xnt, consts)
    if debug:
        dbg_xnt = p.dram_out("dbg_xnt", (128, TPC), BF16)
        dbg_gw = p.dram_out("dbg_gw", (128, E0), BF16)
        p.step("sync", lambda s: p.dma(s, dbg_xnt, xnt[0][:]))
        p.step("sync", lambda s: p.dma(s, dbg_gw, gw_sb[0][:]))
    emit_gate_probs(p, xnt, gw_sb, probsT, consts)
    p.step("sync", lambda s: p.dma(s, probs_out, probsT[:]))
    return p.build()


def run_launch_A(x_shards, inputs):
    p = build_launch_A()
    in_maps = []
    for r in range(NCORE):
        m = {"x": x_shards[r],
             "gate_w": np.ascontiguousarray(inputs["L0_ffn_gate_w"])}
        m.update(const_inmap(_CONST_A))
        in_maps.append(m)
    res = run_prog(p, in_maps)
    return [res[r]["probsT"] for r in range(NCORE)]


def probs_to_bisect_layout(probsT_shards):
    pb = np.empty((128, 1024), np.float32)
    for r in range(NCORE):
        b = r // 4
        j0, j1 = blocks_of(r)
        for e in range(E0):
            pb[b * 64 + e * 8 + j0] = probsT_shards[r][e, :BLK]
            pb[b * 64 + e * 8 + j1] = probsT_shards[r][e, BLK:]
    return pb


_CONST_B = ["ident_bf16", "ident_f32", "G", "GT", "eps_col", "zeros_col",
            "ones_col", "ones_1024"]


def emit_bisect(p, pb, wT, consts, gpre_sb, scratch):
    nc = p.nc
    lo = p.sb("bi_lo", (128, 1), F32)
    hi = p.sb("bi_hi", (128, 1), F32)
    mid = p.sb("bi_mid", (128, 1), F32)
    cnt_p = p.sb("bi_cnt", (128, 1), F32)
    junk = scratch["junk"]
    c16 = p.sb("bi_c16", (1, 16), F32)
    c16T = p.sb("bi_c16T", (16, 1), F32)
    cbc = p.sb("bi_cbc", (128, 1), F32)
    take = p.sb("bi_take", (128, 1), F32)
    tmp1 = p.sb("bi_t1", (128, 1), F32)
    tmp2 = p.sb("bi_t2", (128, 1), F32)
    theta = p.sb("bi_theta", (128, 1), F32)
    thp = p.sb("bi_thp", (128, 1), F32)
    th128 = p.sb("bi_th128", (1, 128), F32)
    th16 = p.sb("bi_th16", (1, 16), F32)
    ngt = p.sb("bi_ngt", (128, 1), F32)
    allow = p.sb("bi_allow", (128, 1), F32)
    eq = scratch["eq"]
    csum = scratch["csum"]
    tbase = p.sb("bi_tbase", (128, 1), F32)
    wsel = scratch["wsel"]
    onesw = scratch["ones"]
    psA = p.ps("bi_psA", (128, 128), F32)

    def init(eng):
        p.dma(eng, lo[:], p.cdram["zeros_col"])
        p.dma(eng, hi[:], p.cdram["ones_col"])
        p.dma(eng, onesw[:], p.cdram["ones_1024"])
    p.step("sync", init)

    def group_sum_to_bc(src_col, dst_col):
        p.step("tensor", lambda t: t.matmul(psA[0:1, 0:16], src_col[:],
                                            consts["G"][:], start=True, stop=True))
        p.step("vector", lambda v: v.tensor_copy(c16[:], psA[0:1, 0:16]))
        p.step("tensor", lambda t: t.transpose(psA[0:16, 0:1], c16[:],
                                               consts["ident_f32"][0:1, 0:1]))
        p.step("vector", lambda v: v.tensor_copy(c16T[:], psA[0:16, 0:1]))
        p.step("tensor", lambda t: t.matmul(psA[:, 0:1], consts["GT"][:],
                                            c16T[:], start=True, stop=True))
        p.step("vector", lambda v: v.tensor_copy(dst_col[:], psA[:, 0:1]))

    def alt(fns):
        for k, fn in enumerate(fns):
            p.step("vector" if k % 2 == 0 else "gpsimd", fn)

    for it in range(BISECT_ITERS):
        p.step("vector", lambda v: v.tensor_scalar(mid[:], lo[:], hi[:], 0.5,
                                                   ALU.add, ALU.mult))
        p.step("vector", lambda v: v.tensor_scalar(junk[:], pb[:], mid[:], 0.0,
                                                   ALU.is_ge, ALU.add,
                                                   accum_out=cnt_p[:]), pad=True)
        group_sum_to_bc(cnt_p, cbc)
        alt([
            lambda v: v.tensor_scalar(take[:], cbc[:], float(CAP), 0.0,
                                      ALU.is_ge, ALU.add),
            lambda g: g.tensor_scalar(tmp1[:], mid[:], lo[:], 1.0,
                                      ALU.subtract, ALU.mult),
            lambda v: v.tensor_scalar(tmp1[:], tmp1[:], take[:], 1.0,
                                      ALU.mult, ALU.mult),
            lambda g: g.tensor_scalar(lo[:], lo[:], tmp1[:], 1.0,
                                      ALU.add, ALU.mult),
            lambda v: v.tensor_scalar(tmp2[:], mid[:], hi[:], 1.0,
                                      ALU.subtract, ALU.mult),
            lambda g: g.tensor_scalar(hi[:], hi[:], tmp2[:], 1.0,
                                      ALU.add, ALU.mult),
            lambda v: v.tensor_scalar(tmp2[:], tmp2[:], take[:], 1.0,
                                      ALU.mult, ALU.mult),
            lambda g: g.tensor_scalar(hi[:], hi[:], tmp2[:], 1.0,
                                      ALU.subtract, ALU.mult),
        ])

    alt([
        lambda v: v.tensor_scalar(junk[:], pb[:], hi[:], 0.0, ALU.is_lt, ALU.add),
        lambda g: g.tensor_tensor(junk[:], junk[:], pb[:], op=ALU.mult),
    ])
    p.step("vector", lambda v: v.tensor_reduce(thp[:], junk[:], axis=AX.X,
                                               op=ALU.max), pad=True)
    p.step("tensor", lambda t: t.transpose(psA[0:1, 0:128], thp[:],
                                           consts["ident_f32"][:]))
    p.step("vector", lambda v: v.tensor_copy(th128[:], psA[0:1, 0:128]))
    p.step("vector", lambda v: v.tensor_reduce(
        th16[:], th128.rearrange("a (g c) -> a g c", c=8),
        axis=AX.X, op=ALU.max), pad=True)
    p.step("tensor", lambda t: t.transpose(psA[0:16, 0:1], th16[:],
                                           consts["ident_f32"][0:1, 0:1]))
    p.step("vector", lambda v: v.tensor_copy(c16T[:], psA[0:16, 0:1]))
    p.step("tensor", lambda t: t.matmul(psA[:, 0:1], consts["GT"][:], c16T[:],
                                        start=True, stop=True))
    p.step("vector", lambda v: v.tensor_copy(theta[:], psA[:, 0:1]))

    p.step("vector", lambda v: v.tensor_scalar(junk[:], pb[:], theta[:], 0.0,
                                               ALU.is_gt, ALU.add,
                                               accum_out=cnt_p[:]), pad=True)
    group_sum_to_bc(cnt_p, ngt)
    p.step("gpsimd", lambda g: g.tensor_scalar(allow[:], ngt[:], -1.0, float(CAP),
                                               ALU.mult, ALU.add))
    p.step("vector", lambda v: v.tensor_scalar(eq[:], pb[:], theta[:], 0.0,
                                               ALU.is_equal, ALU.add,
                                               accum_out=cnt_p[:]), pad=True)
    p.step("vector", lambda v: v.tensor_tensor_scan(csum[:], onesw[:], eq[:],
                                                    p.lnzero[:], ALU.mult,
                                                    ALU.add), pad=True)
    p.step("tensor", lambda t: t.matmul(psA[0:1, 0:128], cnt_p[:], gpre_sb[:],
                                        start=True, stop=True))
    p.step("vector", lambda v: v.tensor_copy(th128[:], psA[0:1, 0:128]))
    p.step("tensor", lambda t: t.transpose(psA[0:128, 0:1], th128[:],
                                           consts["ident_f32"][0:1, 0:1]))
    p.step("vector", lambda v: v.tensor_copy(tbase[:], psA[:, 0:1]))

    alt([
        lambda g: g.tensor_scalar(csum[:], csum[:], tbase[:], 1.0,
                                  ALU.add, ALU.mult),
        lambda v: v.tensor_scalar(csum[:], csum[:], allow[:], 0.0,
                                  ALU.is_le, ALU.add),
        lambda g: g.tensor_tensor(csum[:], csum[:], eq[:], op=ALU.mult),
        lambda v: v.tensor_scalar(wsel[:], pb[:], theta[:], 0.0,
                                  ALU.is_gt, ALU.add),
        lambda g: g.tensor_tensor(wsel[:], wsel[:], csum[:], op=ALU.add),
        lambda v: v.tensor_tensor(wsel[:], wsel[:], pb[:], op=ALU.mult),
    ])
    for j in range(8):
        p.step("tensor", lambda t, j=j: t.transpose(
            psA[:, 0:128], wsel[:, 128 * j:128 * (j + 1)], consts["ident_f32"][:]))
        p.step("vector", lambda v, j=j: v.tensor_copy(
            wT[:, 128 * j:128 * (j + 1)], psA[:, 0:128]))
    p.bisect_state = dict(lo=lo, hi=hi, theta=theta, ngt=ngt, allow=allow,
                          tbase=tbase, cnt_p=cnt_p, cbc=cbc)


def emit_l0ffn(p, xnt, w1_d, w2_d, wT, yacc_d, w1sb, w2sb, hT, stage, scale=1.0):
    psH = p.ps("ff_psH", (128, 512), F32)
    psO = p.ps("ff_psO", (128, 512), F32)

    for e in range(E0):
        def ldw(g, e=e):
            p.dma(g, w1sb[:], w1_d[e].rearrange("(cc pp) h -> pp cc h", pp=128))
            p.dma(g, w2sb[:], w2_d[e].rearrange("(hc pp) c -> pp hc c", pp=128))
        p.step("gpsimd", ldw)
        for tg in range(8):
            for hc in range(32):
                def mmh(t, tg=tg, hc=hc):
                    for cc in range(NCC):
                        last = t.matmul(
                            psH[:, 0:256],
                            w1sb[:, cc * H0 + 128 * hc: cc * H0 + 128 * (hc + 1)],
                            xnt[cc][:, 256 * tg:256 * (tg + 1)],
                            start=(cc == 0), stop=(cc == NCC - 1))
                    return last
                p.step("tensor", mmh)
                p.step("scalar", lambda sc, hc=hc: sc.activation(
                    hT[:, 256 * hc:256 * (hc + 1)], psH[:, 0:256], AF.Gelu,
                    bias=p.lnzero[:], scale=1.0))
            for tt in range(2):
                i = tg * 2 + tt
                for half in range(2):
                    def mmo(t, tt=tt, half=half):
                        for hc in range(32):
                            last = t.matmul(
                                psO[:],
                                hT[:, 256 * hc + 128 * tt: 256 * hc + 128 * (tt + 1)],
                                w2sb[:, hc * C + 512 * half: hc * C + 512 * (half + 1)],
                                start=(hc == 0), stop=(hc == 31))
                        return last
                    p.step("tensor", mmo)
                    wcol = wT[:, 128 * (i % 8) + E0 * 8 * 0 + (0 if i < 8 else 1) +
                              8 * 0:][:, 0:1]
                    col = 128 * (i % 8) + (0 if i < 8 else 1) * 1 + e * 8
                    wcol = wT[:, col:col + 1]
                    p.step("vector", lambda v, wcol=wcol: v.tensor_scalar(
                        stage[:], psO[:], wcol, scale, ALU.mult, ALU.mult))

                    def st(g, e=e, i=i, half=half):
                        dst = yacc_d[128 * i:128 * (i + 1), 512 * half:512 * (half + 1)]
                        if e == 0:
                            p.dma(g, dst, stage[:])
                        else:
                            p.dma(g, dst, stage[:], accum_op=ALU.add)
                    p.step("gpsimd", st)


def build_launch_B(upto="end"):
    p = Prog("B")
    x = p.dram_in("x", (TPC, C))
    probsL = p.dram_in("probsL", (128, 1024))
    gpre_d = p.dram_in("gpre", (128, 128))
    if upto not in ("ln", "bisect"):
        w1_d = p.dram_in("w1", (E0, C, H0))
        w2_d = p.dram_in("w2", (E0, H0, C))
        wkT_d = p.dram_in("wkT", (C, C))
        wvT_d = p.dram_in("wvT", (C, C))
    yacc_d = p.dram_tmp("yacc", (TPC, C))

    MT = 229344 - 32768
    W1O = MT - 65536
    W2O = W1O - 65536
    HTO = W2O - 16384
    xnt = [p.sb_at(f"xnt{cc}", (128, TPC), BF16, MT + 4096 * cc)
           for cc in range(NCC)]
    w1sb = p.sb_at("ff_w1", (128, NCC * H0), BF16, W1O)
    w2sb = p.sb_at("ff_w2", (128, 32 * C), BF16, W2O)
    hT = p.sb_at("ff_hT", (128, 32 * 256), BF16, HTO)
    pb = p.sb_at("pb", (128, 1024), F32, W1O)
    bi_eq = p.sb_at("bi_eq", (128, 1024), F32, W1O + 4096)
    bi_csum = p.sb_at("bi_csum", (128, 1024), F32, W1O + 8192)
    bi_ones = p.sb_at("bi_ones", (128, 1024), F32, W1O + 12288)
    bi_wsel = p.sb_at("bi_wsel", (128, 1024), F32, W1O + 16384)
    xs = p.sb_at("fin_x", (128, 1024), F32, W2O)
    ys = p.sb_at("fin_y", (128, 1024), F32, W2O + 4096)
    x1s = p.sb_at("fin_x1", (128, 1024), F32, W2O + 8192)
    wk_sb = p.sb_at("wk_sb", (128, NCC * C), BF16, W1O)
    wv_sb = p.sb_at("wv_sb", (128, NCC * C), BF16, W1O + 16384)

    consts = setup_consts(p, _CONST_B)
    gpre_sb = p.sb("gpre_sb", (128, 128), F32)
    wT = p.sb("wT", (128, 8 * 128), F32)
    stage = p.sb("ff_stage", (128, 512), F32)
    st = LNState(p)
    st.bind_consts(consts)
    p.lnzero = st.zero

    def ld0(eng):
        p.dma(eng, pb[:], probsL)
        p.dma(eng, gpre_sb[:], gpre_d)
    p.step("sync", ld0)

    dbg_wT = p.dram_out("dbg_wT", (128, 8 * 128))
    emit_ln_all(p, st, x, xnt, consts)
    if upto == "ln":
        p.step("gpsimd", lambda gx: p.dma(gx, dbg_wT[:, 0:128], st.xn[:, 0:128]))
        return p.build()
    emit_bisect(p, pb, wT, consts, gpre_sb,
                scratch=dict(junk=st.junk, eq=bi_eq, csum=bi_csum,
                             ones=bi_ones, wsel=bi_wsel))
    p.step("sync", lambda sx: p.dma(sx, dbg_wT, wT[:]))
    if upto == "bisect":
        dbg_cols = p.dram_out("dbg_cols", (128, 8))
        dc = p.sb("dbg_cols_sb", (128, 8))
        bi = p.bisect_state
        def dump(v):
            for ii, nm in enumerate(["lo", "hi", "theta", "ngt", "allow",
                                     "tbase", "cnt_p", "cbc"]):
                v.tensor_copy(dc[:, ii:ii + 1], bi[nm][:])
            return v.tensor_copy(dc[:, 0:1], bi["lo"][:])
        p.step("vector", dump)
        p.step("sync", lambda sx: p.dma(sx, dbg_cols, dc[:]))
        return p.build()
    emit_l0ffn(p, xnt, w1_d, w2_d, wT, yacc_d, w1sb, w2sb, hT, stage)
    if upto == "ffn":
        ydbg = p.dram_out("dbg_y", (TPC, C))
        for i in range(NTT):
            def ldy(eng, i=i):
                p.dma(eng, ys[:], yacc_d[128 * i:128 * (i + 1), :])
            p.step("sync", ldy)
            p.step("sync", lambda sx, i=i: p.dma(sx, ydbg[128 * i:128 * (i + 1), :], ys[:]))
        return p.build()

    x1_out = p.dram_out("x1", (TPC, C))
    k_out = p.dram_out("K", (TPC, C), BF16)
    v_out = p.dram_out("V", (TPC, C), BF16)
    for i in range(NTT):
        def ld(eng, i=i):
            p.dma(eng, xs[:], x[128 * i:128 * (i + 1), :])
            p.dma(eng, ys[:], yacc_d[128 * i:128 * (i + 1), :])
        p.step("sync", ld)
        p.step("vector", lambda v: v.scalar_tensor_tensor(
            x1s[:], ys[:], 0.5, xs[:], ALU.mult, ALU.add))
        p.step("sync", lambda s, i=i: p.dma(s, x1_out[128 * i:128 * (i + 1), :],
                                            x1s[:]))

    emit_ln_all(p, st, x1_out, xnt, consts)

    def ldw(g):
        p.dma(g, wk_sb[:], wkT_d.rearrange("(cc pp) c -> pp cc c", pp=128))
        p.dma(g, wv_sb[:], wvT_d.rearrange("(cc pp) c -> pp cc c", pp=128))
    p.step("gpsimd", ldw)
    psO2 = p.ps("kv_ps", (128, 512), F32)
    kvs = p.sb("kv_stage", (128, 512), BF16)
    for i in range(NTT):
        for (wsb, outd) in ((wk_sb, k_out), (wv_sb, v_out)):
            for half in range(2):
                def mmkv(t, i=i, wsb=wsb, half=half):
                    for cc in range(NCC):
                        last = t.matmul(
                            psO2[:], xnt[cc][:, 128 * i:128 * (i + 1)],
                            wsb[:, cc * C + 512 * half: cc * C + 512 * (half + 1)],
                            start=(cc == 0), stop=(cc == NCC - 1))
                    return last
                p.step("tensor", mmkv)
                p.step("vector", lambda v: v.tensor_copy(kvs[:], psO2[:]))
                p.step("sync", lambda s, outd=outd, i=i, half=half: p.dma(
                    s, outd[128 * i:128 * (i + 1), 512 * half:512 * (half + 1)],
                    kvs[:]))
    return p.build()


_CONST_C = ["ident_bf16", "ident_f32", "eps_col", "zeros_col",
            "mask0", "mask128", "mask256", "mask384", "ones_1x128_f32"]
KMAXQ = (4, 8)


def build_launch_C():
    p = Prog("C")
    x1 = p.dram_in("x1", (TPC, C))
    kbuf = p.dram_in("Kbuf", (12 * BLK, C), BF16)
    vbuf = p.dram_in("Vbuf", (12 * BLK, NHEAD * 65), BF16)
    cmask_d = p.dram_in("cmask", (128, 128))
    wqT_d = p.dram_in("wqT", (C, C))
    woT_d = p.dram_in("woT", (C, C))
    gw = p.dram_in("gate_w", (C, E0))
    x2_out = p.dram_out("x2", (TPC, C))
    probs_out = p.dram_out("probsT2", (E0, TPC))

    TOP = 229344
    cur = [TOP]

    def man(name, shape, dtype):
        esz = 2 if dtype == BF16 else 4
        sz = -(-shape[-1] * esz // 512) * 512
        cur[0] -= sz
        return p.sb_at(name, list(shape), dtype, cur[0])

    xnt = [man(f"xnt{cc}", (128, TPC), BF16) for cc in range(NCC)]
    QT = [man(f"qt{hp}", (128, BLK), BF16) for hp in range(NCC)]
    KT = [man(f"kt{i}", (128, BLK), BF16) for i in range(8)]
    VA = [man(f"va{i}", (128, NHEAD * 65), BF16) for i in range(8)]
    OACC = [man(f"oacc{h}", (65, 512), F32) for h in range(NHEAD)]
    OT = [man(f"ot{hp}", (128, BLK), BF16) for hp in range(NCC)]
    wsb = man("wsb", (128, NCC * C), BF16)
    PT = man("pt", (128, 8 * 512), BF16)

    consts = setup_consts(p, _CONST_C)
    cmask = p.sb("cmask_sb", (128, 128), F32)
    gw_sb = [p.sb(f"gw{cc}", (128, E0), BF16) for cc in range(NCC)]
    probsT = p.sb("probsT_sb", (E0, TPC), F32)
    x1s = p.sb("c_x1s", (128, 1024), F32)
    x2s = p.sb("c_x2s", (128, 1024), F32)
    rden = p.sb("c_rden", (1, 512), F32)
    st = LNState(p)
    st.bind_consts(consts)
    psS = p.ps("c_psS", (128, 512), F32)
    psPV = p.ps("c_psPV", (65, 512), F32)
    psB = p.ps("c_psB", (64, 512), F32)
    psO = p.ps("c_psO", (128, 512), F32)

    def ld0(eng):
        p.dma(eng, cmask[:], cmask_d)
    p.step("sync", ld0)

    def ldgw(g):
        for cc in range(NCC):
            p.dma(g, gw_sb[cc][:], gw[128 * cc:128 * (cc + 1), :])
    p.step("gpsimd", ldgw)

    emit_ln_all(p, st, x1, xnt, consts)

    for qpos in range(2):
        def ldwq(g):
            p.dma(g, wsb[:], wqT_d.rearrange("(cc pp) c -> pp cc c", pp=128))
        p.step("gpsimd", ldwq)
        for hp in range(NCC):
            for qh in range(2):
                def mmq(t, hp=hp, qh=qh, qpos=qpos):
                    for cc in range(NCC):
                        last = t.matmul(
                            psO[:],
                            wsb[:, cc * C + 128 * hp: cc * C + 128 * (hp + 1)],
                            xnt[cc][:, qpos * BLK + 512 * qh: qpos * BLK + 512 * (qh + 1)],
                            start=(cc == 0), stop=(cc == NCC - 1))
                    return last
                p.step("tensor", mmq)
                p.step("vector", lambda v, hp=hp, qh=qh: v.tensor_copy(
                    QT[hp][:, 512 * qh:512 * (qh + 1)], psO[:]))

        kslot0 = 0 if qpos == 0 else 4
        for qt in range(2):
            for kb in range(KMAXQ[qpos]):
                slot = kslot0 + kb
                diag = (kb == KMAXQ[qpos] - 1)

                def ldk(sc, slot=slot):
                    for dc in range(8):
                        p.dma(sc, KT[dc][:],
                              kbuf[slot * BLK:(slot + 1) * BLK, 128 * dc:128 * (dc + 1)],
                              transpose=True)
                p.step("scalar", ldk)

                def ldv(eng, slot=slot):
                    for kt in range(8):
                        p.dma(eng, VA[kt][:],
                              vbuf[slot * BLK + 128 * kt: slot * BLK + 128 * (kt + 1), :])
                p.step("sync", ldv)

                ktmax = (4 * qt + 4) if diag else 8
                for h in range(NHEAD):
                    hp, hb = h // 2, 64 * (h % 2)
                    for kt in range(ktmax):
                        def mms(t, hp=hp, hb=hb, kt=kt, qt=qt, qpos=qpos):
                            return t.matmul(
                                psS[:], KT[hp][hb:hb + 64, 128 * kt:128 * (kt + 1)],
                                QT[hp][hb:hb + 64, 512 * qt:512 * (qt + 1)],
                                start=True, stop=True)
                        p.step("tensor", mms)
                        if diag and kt >= 4 * qt:
                            d = 128 * kt - 512 * qt
                            p.step("vector", lambda v, d=d: v.tensor_tensor(
                                psS[:], psS[:], consts[f"mask{d}"][:], op=ALU.add))
                        ccol = qpos * 64 + kb * 8 + kt
                        p.step("scalar", lambda sc, kt=kt, ccol=ccol: sc.activation(
                            PT[:, 512 * kt:512 * (kt + 1)], psS[:], AF.Exp,
                            bias=cmask[:, ccol:ccol + 1], scale=0.125))

                    def mpv(t, h=h, ktmax=ktmax):
                        for kt in range(ktmax):
                            last = t.matmul(
                                psPV[:], VA[kt][:, 65 * h:65 * (h + 1)],
                                PT[:, 512 * kt:512 * (kt + 1)],
                                start=(kt == 0), stop=(kt == ktmax - 1))
                        return last
                    p.step("tensor", mpv)
                    if kb == 0:
                        p.step("vector", lambda v, h=h: v.tensor_copy(
                            OACC[h][:], psPV[:]))
                    else:
                        p.step("vector", lambda v, h=h: v.tensor_tensor(
                            OACC[h][:], OACC[h][:], psPV[:], op=ALU.add))

            for h in range(NHEAD):
                hp, hb = h // 2, 64 * (h % 2)
                p.step("vector", lambda v, h=h: v.reciprocal(
                    rden[:], OACC[h][64:65, :]), pad=True)
                p.step("tensor", lambda t: t.matmul(
                    psB[:], consts["ones_1x128_f32"][0:1, 0:64], rden[:],
                    start=True, stop=True))
                p.step("vector", lambda v, h=h, hp=hp, hb=hb, qt=qt: v.tensor_tensor(
                    OT[hp][hb:hb + 64, 512 * qt:512 * (qt + 1)],
                    OACC[h][0:64, :], psB[:], op=ALU.mult))

        def ldwo(g):
            p.dma(g, wsb[:], woT_d.rearrange("(cc pp) c -> pp cc c", pp=128))
        p.step("gpsimd", ldwo)
        for tt in range(8):
            i = qpos * 8 + tt
            def ldx1(eng, i=i):
                p.dma(eng, x1s[:], x1[128 * i:128 * (i + 1), :])
            p.step("sync", ldx1)
            for half in range(2):
                def mmo(t, tt=tt, half=half):
                    for hp in range(NCC):
                        last = t.matmul(
                            psO[:], OT[hp][:, 128 * tt:128 * (tt + 1)],
                            wsb[:, hp * C + 512 * half: hp * C + 512 * (half + 1)],
                            start=(hp == 0), stop=(hp == NCC - 1))
                    return last
                p.step("tensor", mmo)
                p.step("vector", lambda v, half=half: v.tensor_tensor(
                    x2s[:, 512 * half:512 * (half + 1)],
                    x1s[:, 512 * half:512 * (half + 1)], psO[:], op=ALU.add))
            p.step("sync", lambda eng, i=i: p.dma(
                eng, x2_out[128 * i:128 * (i + 1), :], x2s[:]))

    emit_ln_all(p, st, x2_out, xnt, consts)
    emit_gate_probs(p, xnt, gw_sb, probsT, consts)
    p.step("sync", lambda sx: p.dma(sx, probs_out, probsT[:]))
    return p.build()


def host_attn_inputs(K_shards, V_shards, r):
    Kf = unshard_tokens(K_shards)
    Vf = unshard_tokens(V_shards)
    b = r // 4
    kbuf = np.zeros((12 * BLK, C), bfd)
    vbuf = np.zeros((12 * BLK, NHEAD * 65), bfd)
    cmask = np.full((128, 128), -1.25e29, np.float32)
    for qpos, (jp, nslot, s0) in enumerate(
            [(blocks_of(r)[0], 4, 0), (blocks_of(r)[1], 8, 4)]):
        blocks = list(range(jp)) + [None] * (nslot - 1 - jp) + [jp]
        for sidx, blk in enumerate(blocks):
            slot = s0 + sidx
            if blk is None:
                continue
            rows = slice(b * T + blk * BLK, b * T + (blk + 1) * BLK)
            kbuf[slot * BLK:(slot + 1) * BLK] = Kf[rows]
            vsl = vbuf[slot * BLK:(slot + 1) * BLK]
            for h in range(NHEAD):
                vsl[:, 65 * h:65 * h + 64] = Vf[rows][:, 64 * h:64 * (h + 1)]
                vsl[:, 65 * h + 64] = 1.0
            for kt in range(8):
                cmask[:, qpos * 64 + sidx * 8 + kt] = -float(SM_BOUND)
        for sidx, blk in enumerate(blocks):
            if blk is None:
                slot = s0 + sidx
                for h in range(NHEAD):
                    vbuf[slot * BLK:(slot + 1) * BLK, 65 * h + 64] = 1.0
    return kbuf, vbuf, cmask


def run_launch_C(x1_shards, K_shards, V_shards, inputs):
    pC = build_launch_C()
    in_maps = []
    for r in range(NCORE):
        kbuf, vbuf, cmask = host_attn_inputs(K_shards, V_shards, r)
        m = {"x1": x1_shards[r], "Kbuf": kbuf, "Vbuf": vbuf, "cmask": cmask,
             "wqT": np.ascontiguousarray(inputs["L0_attn_Wq"].T),
             "woT": np.ascontiguousarray(inputs["L0_attn_Wo"].T),
             "gate_w": np.ascontiguousarray(inputs["L0_ffn_gate_w"])}
        m.update(const_inmap(_CONST_C))
        in_maps.append(m)
    res = run_prog(pC, in_maps)
    return ([res[r]["x2"] for r in range(NCORE)],
            [res[r]["probsT2"] for r in range(NCORE)])


_CONST_D = ["ident_bf16", "ident_f32", "G", "GT", "eps_col", "zeros_col",
            "ones_col", "ones_1024"]


def emit_l1route(p, xnt, rw_sb, rbbc, WROUT, scr):
    psR = scr["psR"]
    scores, sfc, eqm, tmp, sel, w0 = (scr["scores"], scr["sfc"], scr["eqm"],
                                      scr["tmp"], scr["sel"], scr["w0"])
    m1, m2, gsum, g1, g2, f1, gmask = (scr["m1"], scr["m2"], scr["gsum"],
                                       scr["g1"], scr["g2"], scr["f1"],
                                       scr["gmask"])
    mcol, wsum, winv = scr["mcol"], scr["wsum"], scr["winv"]

    def g3(ap):
        return ap.rearrange("p (g e) -> p g e", e=8)

    for i in range(NTT):
        def mmr(t, i=i):
            for cc in range(NCC):
                last = t.matmul(psR[:], xnt[cc][:, 128 * i:128 * (i + 1)],
                                rw_sb[cc][:], start=(cc == 0), stop=(cc == NCC - 1))
            return last
        p.step("tensor", mmr)
        p.step("scalar", lambda sc: sc.activation(scores[:], psR[:], AF.Sigmoid,
                                                  bias=p.lnzero[:], scale=1.0))
        p.step("vector", lambda v: v.tensor_tensor(sfc[:], scores[:], rbbc[:],
                                                   op=ALU.add))
        p.step("vector", lambda v: v.tensor_reduce(m1[:], g3(sfc), axis=AX.X,
                                                   op=ALU.max), pad=True)
        p.step("vector", lambda v: v.tensor_tensor(
            g3(eqm), g3(sfc), m1.unsqueeze(2).broadcast_to([128, 4, 8]),
            op=ALU.is_equal))
        p.step("gpsimd", lambda g: g.tensor_scalar(eqm[:], eqm[:], -1e30, 0.0,
                                                   ALU.mult, ALU.add))
        p.step("vector", lambda v: v.tensor_tensor(eqm[:], eqm[:], sfc[:],
                                                   op=ALU.add))
        p.step("vector", lambda v: v.tensor_reduce(m2[:], g3(eqm), axis=AX.X,
                                                   op=ALU.max), pad=True)
        p.step("gpsimd", lambda g: g.tensor_tensor(gsum[:], m1[:], m2[:],
                                                   op=ALU.add))
        p.step("vector", lambda v: v.tensor_reduce(g1[:], gsum[:], axis=AX.X,
                                                   op=ALU.max), pad=True)
        p.step("vector", lambda v: v.tensor_scalar(f1[:], gsum[:], g1[:], 0.0,
                                                   ALU.is_equal, ALU.add))
        p.step("gpsimd", lambda g: g.tensor_scalar(f1[:], f1[:], -1e30, 0.0,
                                                   ALU.mult, ALU.add))
        p.step("vector", lambda v: v.tensor_tensor(f1[:], f1[:], gsum[:],
                                                   op=ALU.add))
        p.step("vector", lambda v: v.tensor_reduce(g2[:], f1[:], axis=AX.X,
                                                   op=ALU.max), pad=True)
        p.step("vector", lambda v: v.tensor_scalar(gmask[:], gsum[:], g2[:], 0.0,
                                                   ALU.is_ge, ALU.add))
        p.step("vector", lambda v: v.tensor_copy(
            g3(tmp), gmask.unsqueeze(2).broadcast_to([128, 4, 8])))
        p.step("gpsimd", lambda g: g.tensor_tensor(tmp[:], tmp[:], sfc[:],
                                                   op=ALU.mult))
        for rnd in range(4):
            p.step("vector", lambda v: v.tensor_reduce(mcol[:], tmp[:], axis=AX.X,
                                                       op=ALU.max), pad=True)
            p.step("vector", lambda v: v.tensor_scalar(eqm[:], tmp[:], mcol[:],
                                                       0.0, ALU.is_equal, ALU.add))
            if rnd == 0:
                p.step("gpsimd", lambda g: g.tensor_copy(sel[:], eqm[:]))
            else:
                p.step("vector", lambda v: v.tensor_tensor(sel[:], sel[:], eqm[:],
                                                           op=ALU.max))
            p.step("vector", lambda v: v.tensor_scalar(eqm[:], eqm[:], -1e30, 0.0,
                                                       ALU.mult, ALU.add))
            p.step("gpsimd", lambda g: g.tensor_tensor(tmp[:], tmp[:], eqm[:],
                                                       op=ALU.add))
        p.step("vector", lambda v: v.tensor_tensor(w0[:], scores[:], sel[:],
                                                   op=ALU.mult))
        p.step("vector", lambda v: v.tensor_reduce(wsum[:], w0[:], axis=AX.X,
                                                   op=ALU.add), pad=True)
        p.step("gpsimd", lambda g: g.tensor_scalar(wsum[:], wsum[:], 1.0, 1e-20,
                                                   ALU.mult, ALU.add))
        p.step("vector", lambda v: v.reciprocal(winv[:], wsum[:]), pad=True)
        p.step("gpsimd", lambda g, i=i: g.tensor_scalar(
            WROUT[:, 32 * i:32 * (i + 1)], w0[:], winv[:], 2.5, ALU.mult, ALU.mult))


def emit_l1ffn(p, xnt, weights_d, WROUT, yacc_d, bufs):
    sgT_d, suT_d, sdT_d, gwT_d, uwT_d, dwT_d = weights_d
    wA, wB, wC, hTs, hTr, stage, gsb = (bufs["wA"], bufs["wB"], bufs["wC"],
                                        bufs["hTs"], bufs["hTr"], bufs["stage"],
                                        bufs["gsb"])
    psH = p.ps("l1_psH", (128, 512), F32)
    psO = p.ps("l1_psO", (128, 512), F32)

    def ldsh(g):
        p.dma(g, wA[:], sgT_d.rearrange("(cc pp) i -> pp cc i", pp=128))
        p.dma(g, wB[:], suT_d.rearrange("(cc pp) i -> pp cc i", pp=128))
        p.dma(g, wC[:], sdT_d.rearrange("(ic pp) c -> pp ic c", pp=128))
    p.step("gpsimd", ldsh)
    for tg in range(4):
        for ih in range(8):
            def mmg(t, tg=tg, ih=ih):
                for cc in range(NCC):
                    last = t.matmul(
                        psH[:], wA[:, cc * SHI + 128 * ih: cc * SHI + 128 * (ih + 1)],
                        xnt[cc][:, 512 * tg:512 * (tg + 1)],
                        start=(cc == 0), stop=(cc == NCC - 1))
                return last
            p.step("tensor", mmg)
            p.step("scalar", lambda sc: sc.activation(gsb[:], psH[:], AF.Silu,
                                                      bias=p.lnzero[:], scale=1.0))
            def mmu(t, tg=tg, ih=ih):
                for cc in range(NCC):
                    last = t.matmul(
                        psH[:], wB[:, cc * SHI + 128 * ih: cc * SHI + 128 * (ih + 1)],
                        xnt[cc][:, 512 * tg:512 * (tg + 1)],
                        start=(cc == 0), stop=(cc == NCC - 1))
                return last
            p.step("tensor", mmu)
            p.step("vector", lambda v, ih=ih: v.tensor_tensor(
                hTs[:, 512 * ih:512 * (ih + 1)], gsb[:], psH[:], op=ALU.mult))
        for tt in range(4):
            i = tg * 4 + tt
            for half in range(2):
                def mmo(t, tt=tt, half=half):
                    for ih in range(8):
                        last = t.matmul(
                            psO[:],
                            hTs[:, 512 * ih + 128 * tt:
                                 512 * ih + 128 * (tt + 1)],
                            wC[:, ih * C + 512 * half: ih * C + 512 * (half + 1)],
                            start=(ih == 0), stop=(ih == 7))
                    return last
                p.step("tensor", mmo)
                p.step("vector", lambda v: v.tensor_copy(stage[:], psO[:]))
                def st(g, i=i, half=half):
                    p.dma(g, yacc_d[128 * i:128 * (i + 1),
                                    512 * half:512 * (half + 1)], stage[:])
                p.step("gpsimd", st)

    for e in range(E1):
        def lde(g, e=e):
            p.dma(g, wA[:, 0:NCC * I1],
                  gwT_d[e].rearrange("(cc pp) i -> pp cc i", pp=128))
            p.dma(g, wB[:, 0:NCC * I1],
                  uwT_d[e].rearrange("(cc pp) i -> pp cc i", pp=128))
            p.dma(g, wC[:, 0:4 * C],
                  dwT_d[e].rearrange("(ic pp) c -> pp ic c", pp=128))
        p.step("gpsimd", lde)
        for tg in range(4):
            for it in range(4):
                def mmg(t, tg=tg, it=it):
                    for cc in range(NCC):
                        last = t.matmul(
                            psH[:], wA[:, cc * I1 + 128 * it: cc * I1 + 128 * (it + 1)],
                            xnt[cc][:, 512 * tg:512 * (tg + 1)],
                            start=(cc == 0), stop=(cc == NCC - 1))
                    return last
                p.step("tensor", mmg)
                p.step("scalar", lambda sc: sc.activation(gsb[:], psH[:], AF.Silu,
                                                          bias=p.lnzero[:], scale=1.0))
                def mmu(t, tg=tg, it=it):
                    for cc in range(NCC):
                        last = t.matmul(
                            psH[:], wB[:, cc * I1 + 128 * it: cc * I1 + 128 * (it + 1)],
                            xnt[cc][:, 512 * tg:512 * (tg + 1)],
                            start=(cc == 0), stop=(cc == NCC - 1))
                    return last
                p.step("tensor", mmu)
                p.step("vector", lambda v, it=it: v.tensor_tensor(
                    hTr[:, 512 * it:512 * (it + 1)], gsb[:], psH[:], op=ALU.mult))
            for tt in range(4):
                i = tg * 4 + tt
                for half in range(2):
                    def mmo(t, tt=tt, half=half):
                        for it in range(4):
                            last = t.matmul(
                                psO[:],
                                hTr[:, 512 * it + 128 * tt: 512 * it + 128 * (tt + 1)],
                                wC[:, it * C + 512 * half: it * C + 512 * (half + 1)],
                                start=(it == 0), stop=(it == 3))
                        return last
                    p.step("tensor", mmo)
                    wcol = WROUT[:, 32 * i + e: 32 * i + e + 1]
                    p.step("vector", lambda v, wcol=wcol: v.tensor_scalar(
                        stage[:], psO[:], wcol, 1.0, ALU.mult, ALU.mult))
                    def st(g, i=i, half=half):
                        p.dma(g, yacc_d[128 * i:128 * (i + 1),
                                        512 * half:512 * (half + 1)], stage[:],
                              accum_op=ALU.add)
                    p.step("gpsimd", st)


def build_launch_D():
    p = Prog("D")
    x2 = p.dram_in("x2", (TPC, C))
    probsL = p.dram_in("probsL", (128, 1024))
    gpre_d = p.dram_in("gpre", (128, 128))
    w1_d = p.dram_in("w1", (E0, C, H0))
    w2_d = p.dram_in("w2", (E0, H0, C))
    rwT_d = p.dram_in("rwT", (C, E1))
    rbbc_d = p.dram_in("rb_bc", (128, E1))
    sgT_d = p.dram_in("sgT", (C, SHI))
    suT_d = p.dram_in("suT", (C, SHI))
    sdT_d = p.dram_in("sdT", (SHI, C))
    gwT_d = p.dram_in("gwT", (E1, C, I1))
    uwT_d = p.dram_in("uwT", (E1, C, I1))
    dwT_d = p.dram_in("dwT", (E1, I1, C))
    x4_out = p.dram_out("x4", (TPC, C))
    bounds_out = p.dram_out("bounds", (2, C), BF16)
    yacc_d = p.dram_tmp("yacc", (TPC, C))
    x3_d = p.dram_tmp("x3", (TPC, C))
    yacc2_d = p.dram_tmp("yacc2", (TPC, C))

    MT = 229344 - 32768
    W1O = MT - 65536
    W2O = W1O - 65536
    HTO = W2O - 16384
    xnt = [p.sb_at(f"xnt{cc}", (128, TPC), BF16, MT + 4096 * cc)
           for cc in range(NCC)]
    w1sb = p.sb_at("ff_w1", (128, NCC * H0), BF16, W1O)
    w2sb = p.sb_at("ff_w2", (128, 32 * C), BF16, W2O)
    hT = p.sb_at("ff_hT", (128, 32 * 256), BF16, HTO)
    pb = p.sb_at("pb", (128, 1024), F32, W1O)
    bi_eq = p.sb_at("bi_eq", (128, 1024), F32, W1O + 4096)
    bi_csum = p.sb_at("bi_csum", (128, 1024), F32, W1O + 8192)
    bi_ones = p.sb_at("bi_ones", (128, 1024), F32, W1O + 12288)
    bi_wsel = p.sb_at("bi_wsel", (128, 1024), F32, W1O + 16384)
    xs = p.sb_at("fin_x", (128, 1024), F32, W2O)
    ys = p.sb_at("fin_y", (128, 1024), F32, W2O + 4096)
    x1s = p.sb_at("fin_o", (128, 1024), F32, W2O + 8192)
    wA = p.sb_at("l1_wA", (128, NCC * SHI), BF16, W1O)
    wB = p.sb_at("l1_wB", (128, NCC * SHI), BF16, W1O + 16384)
    wC = p.sb_at("l1_wC", (128, NCC * C), BF16, W1O + 32768)
    hTs = p.sb_at("l1_hTs", (128, 8 * 512), BF16, W2O + 16384)
    hTr = p.sb_at("l1_hTr", (128, 4 * 512), BF16, W2O + 24576)

    consts = setup_consts(p, _CONST_D)
    gpre_sb = p.sb("gpre_sb", (128, 128), F32)
    wT = p.sb("wT", (128, 8 * 128), F32)
    stage = p.sb("ff_stage", (128, 512), F32)
    gsb = p.sb("l1_gsb", (128, 512), BF16)
    rw_sb = [p.sb(f"rw{cc}", (128, E1), BF16) for cc in range(NCC)]
    rbbc = p.sb("rbbc", (128, E1), F32)
    WROUT = p.sb("wrout", (128, NTT * E1), F32)
    scr = dict(psR=p.ps("rt_ps", (128, E1), F32),
               scores=p.sb("rt_sc", (128, E1), F32),
               sfc=p.sb("rt_sfc", (128, E1), F32),
               eqm=p.sb("rt_eqm", (128, E1), F32),
               tmp=p.sb("rt_tmp", (128, E1), F32),
               sel=p.sb("rt_sel", (128, E1), F32),
               w0=p.sb("rt_w0", (128, E1), F32),
               m1=p.sb("rt_m1", (128, 4), F32),
               m2=p.sb("rt_m2", (128, 4), F32),
               gsum=p.sb("rt_gs", (128, 4), F32),
               g1=p.sb("rt_g1", (128, 1), F32),
               g2=p.sb("rt_g2", (128, 1), F32),
               f1=p.sb("rt_f1", (128, 4), F32),
               gmask=p.sb("rt_gm", (128, 4), F32),
               mcol=p.sb("rt_mc", (128, 1), F32),
               wsum=p.sb("rt_ws", (128, 1), F32),
               winv=p.sb("rt_wi", (128, 1), F32))
    st = LNState(p)
    st.bind_consts(consts)
    p.lnzero = st.zero

    def ld0(eng):
        p.dma(eng, pb[:], probsL)
        p.dma(eng, gpre_sb[:], gpre_d)
        p.dma(eng, rbbc[:], rbbc_d)
    p.step("sync", ld0)

    emit_ln_all(p, st, x2, xnt, consts)
    emit_bisect(p, pb, wT, consts, gpre_sb,
                scratch=dict(junk=st.junk, eq=bi_eq, csum=bi_csum,
                             ones=bi_ones, wsel=bi_wsel))
    emit_l0ffn(p, xnt, w1_d, w2_d, wT, yacc_d, w1sb, w2sb, hT, stage)
    for i in range(NTT):
        def ld(eng, i=i):
            p.dma(eng, xs[:], x2[128 * i:128 * (i + 1), :])
            p.dma(eng, ys[:], yacc_d[128 * i:128 * (i + 1), :])
        p.step("sync", ld)
        p.step("vector", lambda v: v.scalar_tensor_tensor(
            x1s[:], ys[:], 0.5, xs[:], ALU.mult, ALU.add))
        p.step("sync", lambda eng, i=i: p.dma(eng, x3_d[128 * i:128 * (i + 1), :],
                                              x1s[:]))

    emit_ln_all(p, st, x3_d, xnt, consts)

    def ldrw(g):
        for cc in range(NCC):
            p.dma(g, rw_sb[cc][:], rwT_d[128 * cc:128 * (cc + 1), :])
    p.step("gpsimd", ldrw)
    emit_l1route(p, xnt, rw_sb, rbbc, WROUT, scr)
    emit_l1ffn(p, xnt, (sgT_d, suT_d, sdT_d, gwT_d, uwT_d, dwT_d), WROUT,
               yacc2_d, dict(wA=wA, wB=wB, wC=wC, hTs=hTs, hTr=hTr,
                             stage=stage, gsb=gsb))
    for i in range(NTT):
        def ld(eng, i=i):
            p.dma(eng, xs[:], x3_d[128 * i:128 * (i + 1), :])
            p.dma(eng, ys[:], yacc2_d[128 * i:128 * (i + 1), :])
        p.step("sync", ld)
        p.step("vector", lambda v: v.scalar_tensor_tensor(
            x1s[:], ys[:], 0.5, xs[:], ALU.mult, ALU.add))
        p.step("sync", lambda eng, i=i: p.dma(eng, x4_out[128 * i:128 * (i + 1), :],
                                              x1s[:]))

    for bi, i in enumerate((7, 15)):
        emit_ln_tile(p, st, x4_out, i, xnt, consts)
        p.step("sync", lambda eng, bi=bi: p.dma(
            eng, bounds_out[bi:bi + 1, :], st.xn[127:128, :]))
    return p.build()


def run_launch_D(x2_shards, inputs, probsT2_shards):
    pbe = host_probs_be(probsT2_shards)
    pD = build_launch_D()
    tw = dict(
        w1=inputs["L0_ffn_w1"], w2=inputs["L0_ffn_w2"],
        rwT=np.ascontiguousarray(inputs["L1_ffn_router_weight"].T),
        rb_bc=np.ascontiguousarray(
            np.broadcast_to(inputs["L1_ffn_router_bias"], (128, E1))),
        sgT=np.ascontiguousarray(inputs["L1_ffn_shared_gate_w"].T),
        suT=np.ascontiguousarray(inputs["L1_ffn_shared_up_w"].T),
        sdT=np.ascontiguousarray(inputs["L1_ffn_shared_down_w"].T),
        gwT=np.ascontiguousarray(inputs["L1_ffn_gate_w"].transpose(0, 2, 1)),
        uwT=np.ascontiguousarray(inputs["L1_ffn_up_w"].transpose(0, 2, 1)),
        dwT=np.ascontiguousarray(inputs["L1_ffn_down_w"].transpose(0, 2, 1)))
    in_maps = []
    for r in range(NCORE):
        pl, gpre = host_bisect_inputs(pbe, r)
        m = {"x2": x2_shards[r], "probsL": pl, "gpre": gpre}
        m.update(tw)
        m.update(const_inmap(_CONST_D))
        in_maps.append(m)
    res = run_prog(pD, in_maps)
    return ([res[r]["x4"] for r in range(NCORE)],
            [res[r]["bounds"] for r in range(NCORE)])


_CONST_E = ["ident_bf16", "ident_f32", "eps_col", "zeros_col", "ones_1024"]
_CONST_F = ["ident_bf16", "ident_f32", "G", "GT", "eps_col", "zeros_col",
            "ones_col", "ones_1024"]


def emit_wkv(p, xnt, finals_only, dram, bufs, consts, x4_dram, x5_dram):
    wkT_d, wvT_d = dram["wk"], dram["wv"]
    xk, xv = bufs["xk"], bufs["xv"]
    xr = bufs.get("xr")
    tm5 = bufs["tm5"]
    um3 = bufs["um3"]
    bq = bufs["bq"]
    wcc = bufs["wcc"]
    B1, B2, B3, B4, B5, B6 = bufs["scratch"]
    B7 = bufs["B7"]
    lamstore = bufs["lamstore"]
    lam1024 = bufs["lam1024"]
    psK = p.ps("wk_psK", (128, 512), F32)

    p.step("vector", lambda v: v.tensor_scalar(um3[:], tm5[:, 0:24], -1.0, 1.0,
                                               ALU.mult, ALU.add))
    for cc in range(NCC):
        tdc = tm5[:, 24 + cc:24 + cc + 1]
        p.step("scalar", lambda sc, tdc=tdc: sc.activation(
            B1[:, 0:1], tdc, AF.Exp, bias=p.lnzero[:], scale=1.0))
        p.step("gpsimd", lambda g: g.tensor_scalar(
            B1[:, 1:2], B1[:, 0:1], -1.0, 0.0, ALU.mult, ALU.add))
        p.step("scalar", lambda sc, cc=cc: sc.activation(
            lamstore[:, cc:cc + 1], B1[:, 1:2], AF.Exp, bias=p.lnzero[:],
            scale=1.0))
        p.step("gpsimd", lambda g, cc=cc: g.tensor_scalar(
            bufs["wvec"][:, cc:cc + 1], B1[:, 1:2], 1.0, 0.0, ALU.mult, ALU.add))

    if not finals_only:
        emit_state_chain(p, bufs)

    for bb in range(2):
        mixes = [(xk, 0), (xv, 1)] + ([(xr, 2)] if not finals_only else [])
        for ic in range(NCC):
            src = xnt[ic][:, 1024 * bb:1024 * (bb + 1)]
            for dst, tmi in mixes:
                tcol = tm5[:, tmi * 8 + ic:tmi * 8 + ic + 1]
                ucol = um3[:, tmi * 8 + ic:tmi * 8 + ic + 1]
                bcol = bq[:, bb * 8 + ic:bb * 8 + ic + 1]
                d = dst[ic]
                p.step("vector", lambda v, d=d, src=src, ucol=ucol: v.tensor_scalar(
                    d[:, 1:1024], src[:, 0:1023], ucol, 1.0, ALU.mult, ALU.mult))
                p.step("gpsimd", lambda g, d=d, bcol=bcol, ucol=ucol: g.tensor_scalar(
                    d[:, 0:1], bcol, ucol, 1.0, ALU.mult, ALU.mult))
                p.step("vector", lambda v, src=src, tcol=tcol: v.tensor_scalar(
                    B7[:], src, tcol, 1.0, ALU.mult, ALU.mult))
                p.step("gpsimd", lambda g, d=d: g.tensor_tensor(
                    d[:], d[:], B7[:], op=ALU.add))

        if not finals_only:
            for cc in range(NCC):
                def ldwr(g, cc=cc):
                    p.dma(g, wcc[:, 0:1024],
                          dram["wr"][:, 128 * cc:128 * (cc + 1)].rearrange(
                              "(ic pp) c -> pp ic c", pp=128))
                p.step("gpsimd", ldwr)
                for tg in range(2):
                    def mmr(t, tg=tg):
                        for ic in range(NCC):
                            last = t.matmul(
                                psK[:], wcc[:, 128 * ic:128 * (ic + 1)],
                                xr[ic][:, 512 * tg:512 * (tg + 1)],
                                start=(ic == 0), stop=(ic == NCC - 1))
                        return last
                    p.step("tensor", mmr)
                    p.step("scalar", lambda sc, cc=cc, tg=tg: sc.activation(
                        bufs["srT"][cc][:, 512 * tg:512 * (tg + 1)], psK[:],
                        AF.Sigmoid, bias=p.lnzero[:], scale=1.0))

        for cc in range(NCC):
            def ldw(g, cc=cc):
                p.dma(g, wcc[:, 0:1024],
                      wkT_d[:, 128 * cc:128 * (cc + 1)].rearrange(
                          "(ic pp) c -> pp ic c", pp=128))
                p.dma(g, wcc[:, 1024:2048],
                      wvT_d[:, 128 * cc:128 * (cc + 1)].rearrange(
                          "(ic pp) c -> pp ic c", pp=128))
            p.step("gpsimd", ldw)
            p.step("vector", lambda v, cc=cc: v.tensor_scalar(
                lam1024[:], consts["ones_1024"][:], lamstore[:, cc:cc + 1], 1.0,
                ALU.mult, ALU.mult))
            for woff, dstf, srcx in ((0, B1, xk), (1024, B4, xv)):
                for tg in range(2):
                    def mmk(t, woff=woff, tg=tg, srcx=srcx):
                        for ic in range(NCC):
                            last = t.matmul(
                                psK[:], wcc[:, woff + 128 * ic:woff + 128 * (ic + 1)],
                                srcx[ic][:, 512 * tg:512 * (tg + 1)],
                                start=(ic == 0), stop=(ic == NCC - 1))
                        return last
                    p.step("tensor", mmk)
                    p.step("vector", lambda v, dstf=dstf, tg=tg: v.tensor_copy(
                        dstf[:, 512 * tg:512 * (tg + 1)], psK[:]))
            ufc = tm5[:, 32 + cc:32 + cc + 1]
            p.step("scalar", lambda sc: sc.activation(B2[:], B1[:], AF.Exp,
                                                      bias=p.lnzero[:], scale=1.0))
            if not finals_only:
                p.step("scalar", lambda sc, ufc=ufc: sc.activation(
                    B3[:], B1[:], AF.Exp, bias=ufc, scale=1.0))
            p.step("vector", lambda v: v.tensor_tensor(B5[:], B2[:], B4[:],
                                                       op=ALU.mult))
            if finals_only:
                ia = consts["zeros_col"][:]
                ib = consts["zeros_col"][:]
            else:
                ia = bufs["sina"][:, bb * 8 + cc:bb * 8 + cc + 1]
                ib = bufs["sinb"][:, bb * 8 + cc:bb * 8 + cc + 1]
            p.step("vector", lambda v, ib=ib: v.tensor_tensor_scan(
                B6[:], lam1024[:], B2[:], ib, ALU.mult, ALU.add), pad=True)
            p.step("vector", lambda v, ia=ia: v.tensor_tensor_scan(
                B1[:], lam1024[:], B5[:], ia, ALU.mult, ALU.add), pad=True)
            if finals_only:
                p.step("gpsimd", lambda g, cc=cc, bb=bb: g.tensor_scalar(
                    bufs["finals"][:, cc * 4 + bb * 2:cc * 4 + bb * 2 + 1],
                    B1[:, 1023:1024], 1.0, 0.0, ALU.mult, ALU.add))
                p.step("gpsimd", lambda g, cc=cc, bb=bb: g.tensor_scalar(
                    bufs["finals"][:, cc * 4 + bb * 2 + 1:cc * 4 + bb * 2 + 2],
                    B6[:, 1023:1024], 1.0, 0.0, ALU.mult, ALU.add))
                continue
            p.step("vector", lambda v: v.tensor_tensor(B5[:], B3[:], B4[:],
                                                       op=ALU.mult))
            p.step("gpsimd", lambda g: g.tensor_tensor(
                B5[:, 1:1024], B5[:, 1:1024], B1[:, 0:1023], op=ALU.add))
            p.step("vector", lambda v, ia=ia: v.tensor_scalar(
                B5[:, 0:1], B5[:, 0:1], ia, 1.0, ALU.add, ALU.mult))
            p.step("vector", lambda v: v.tensor_copy(B4[:], B3[:]))
            p.step("gpsimd", lambda g: g.tensor_tensor(
                B4[:, 1:1024], B4[:, 1:1024], B6[:, 0:1023], op=ALU.add))
            p.step("vector", lambda v, ib=ib: v.tensor_scalar(
                B4[:, 0:1], B4[:, 0:1], ib, 1.0, ALU.add, ALU.mult))
            p.step("vector", lambda v: v.reciprocal(B2[:], B4[:]), pad=True)
            p.step("gpsimd", lambda g: g.tensor_tensor(B5[:], B5[:], B2[:],
                                                       op=ALU.mult))
            p.step("vector", lambda v: v.tensor_copy(B7[:], B5[:]))
            p.step("gpsimd", lambda g, cc=cc: g.tensor_tensor(
                bufs["rwkv"][cc][:], B7[:], bufs["srT"][cc][:], op=ALU.mult))

        if not finals_only:
            def ldwo(g):
                p.dma(g, bufs["wo_sb"][:],
                      dram["wo"].rearrange("(cc pp) c -> pp cc c", pp=128))
            p.step("gpsimd", ldwo)
            for tt in range(8):
                i = bb * 8 + tt
                def ldx4(eng, i=i):
                    p.dma(eng, bufs["x4s"][:], x4_dram[128 * i:128 * (i + 1), :])
                p.step("sync", ldx4)
                for half in range(2):
                    def mmo(t, tt=tt, half=half):
                        for cc in range(NCC):
                            last = t.matmul(
                                psK[:], bufs["rwkv"][cc][:, 128 * tt:128 * (tt + 1)],
                                bufs["wo_sb"][:, cc * C + 512 * half:
                                              cc * C + 512 * (half + 1)],
                                start=(cc == 0), stop=(cc == NCC - 1))
                        return last
                    p.step("tensor", mmo)
                    p.step("vector", lambda v, half=half: v.tensor_tensor(
                        bufs["x5s"][:, 512 * half:512 * (half + 1)],
                        bufs["x4s"][:, 512 * half:512 * (half + 1)], psK[:],
                        op=ALU.add))
                p.step("sync", lambda eng, i=i: p.dma(
                    eng, x5_dram[128 * i:128 * (i + 1), :], bufs["x5s"][:]))


def emit_state_chain(p, bufs):
    finsel, cpow = bufs["finsel"], bufs["cpow"]
    wvec = bufs["wvec"]
    t1, t2, t3 = bufs["ch_t1"], bufs["ch_t2"], bufs["ch_t3"]
    for bb in range(2):
        for cc in range(NCC):
            p.step("vector", lambda v, bb=bb, cc=cc: v.tensor_scalar(
                t1[:], cpow[:, (bb * 8 + cc) * 8:(bb * 8 + cc) * 8 + 8],
                wvec[:, cc:cc + 1], 1.0, ALU.mult, ALU.mult))
            p.step("scalar", lambda sc: sc.activation(t2[:], t1[:], AF.Exp,
                                                      bias=p.lnzero[:], scale=1.0))
            for ab, dst in ((0, bufs["sina"]), (1, bufs["sinb"])):
                off = (bb * 8 + cc) * 16 + ab * 8
                p.step("vector", lambda v, off=off: v.tensor_tensor(
                    t3[:], t2[:], finsel[:, off:off + 8], op=ALU.mult))
                p.step("vector", lambda v, dst=dst, bb=bb, cc=cc: v.tensor_reduce(
                    dst[:, bb * 8 + cc:bb * 8 + cc + 1], t3[:], axis=AX.X,
                    op=ALU.add), pad=True)


def make_wkv_bufs(p, finals_only, man):
    bufs = {}
    bufs["xk"] = [man(f"xk{i}", (128, 1024), BF16) for i in range(NCC)]
    bufs["xv"] = [man(f"xv{i}", (128, 1024), BF16) for i in range(NCC)]
    if not finals_only:
        bufs["xr"] = [man(f"xr{i}", (128, 1024), BF16) for i in range(NCC)]
        bufs["srT"] = [man(f"sr{i}", (128, 1024), BF16) for i in range(NCC)]
        bufs["rwkv"] = [man(f"rk{i}", (128, 1024), BF16) for i in range(NCC)]
        bufs["wo_sb"] = man("wo_sb", (128, NCC * C), BF16)
        bufs["x4s"] = p.sb("x4s", (128, 1024), F32)
        bufs["x5s"] = p.sb("x5s", (128, 1024), F32)
        bufs["sina"] = p.sb("sina", (128, 16), F32)
        bufs["sinb"] = p.sb("sinb", (128, 16), F32)
        bufs["finsel"] = p.sb("finsel_sb", (128, 256), F32)
        bufs["cpow"] = p.sb("cpow_sb", (128, 128), F32)
        bufs["ch_t1"] = p.sb("ch_t1", (128, 8), F32)
        bufs["ch_t2"] = p.sb("ch_t2", (128, 8), F32)
        bufs["ch_t3"] = p.sb("ch_t3", (128, 8), F32)
    bufs["scratch"] = [man(f"wkB{i}", (128, 1024), F32) for i in range(6)]
    bufs["B7"] = man("wkB7", (128, 1024), BF16)
    bufs["tm5"] = p.sb("tm5_sb", (128, 40), F32)
    bufs["um3"] = p.sb("um3", (128, 24), F32)
    bufs["bq"] = p.sb("bq", (128, 16), F32)
    bufs["wcc"] = p.sb("wcc", (128, 2048), BF16)
    bufs["lamstore"] = p.sb("lamstore", (128, 8), F32)
    bufs["wvec"] = p.sb("wvec", (128, 8), F32)
    bufs["lam1024"] = man("lam1024", (128, 1024), F32)
    bufs["finals"] = p.sb("finals_sb", (128, 32), F32)
    return bufs


def _load_wkv_vectors(p, bufs, tm5_d, bounds_d):
    def ld(g):
        p.dma(g, bufs["tm5"][:], tm5_d)
        p.dma(g, bufs["bq"][:], bounds_d)
    p.step("gpsimd", ld)


def build_launch_E():
    p = Prog("E")
    x4 = p.dram_in("x4", (TPC, C))
    bounds_d = p.dram_in("bounds", (128, 16), BF16)
    tm5_d = p.dram_in("tm5", (128, 40))
    wkT_d = p.dram_in("wkT", (C, C))
    wvT_d = p.dram_in("wvT", (C, C))
    finals_out = p.dram_out("finals", (128, 32))

    TOPm = [229344]

    def man(name, shape, dtype):
        esz = 2 if dtype == BF16 else 4
        TOPm[0] -= -(-shape[-1] * esz // 512) * 512
        return p.sb_at(name, list(shape), dtype, TOPm[0])

    xnt = [man(f"xnt{cc}", (128, TPC), BF16) for cc in range(NCC)]
    bufs = make_wkv_bufs(p, True, man)
    consts = setup_consts(p, _CONST_E)
    bufs["consts"] = consts
    st = LNState(p)
    st.bind_consts(consts)
    p.lnzero = st.zero
    _load_wkv_vectors(p, bufs, tm5_d, bounds_d)
    emit_ln_all(p, st, x4, xnt, consts)
    emit_wkv(p, xnt, True, dict(wk=wkT_d, wv=wvT_d), bufs, consts, None, None)
    p.step("sync", lambda sx: p.dma(sx, finals_out, bufs["finals"][:]))
    return p.build()


def build_launch_F():
    p = Prog("F")
    x4 = p.dram_in("x4", (TPC, C))
    bounds_d = p.dram_in("bounds", (128, 16), BF16)
    tm5_d = p.dram_in("tm5", (128, 40))
    wkT_d = p.dram_in("wkT", (C, C))
    wvT_d = p.dram_in("wvT", (C, C))
    wrT_d = p.dram_in("wrT", (C, C))
    woT_d = p.dram_in("woT", (C, C))
    finsel_d = p.dram_in("finsel", (128, 256))
    cpow_d = p.dram_in("cpow", (128, 128))
    rwT_d = p.dram_in("rwT", (C, E1))
    rbbc_d = p.dram_in("rb_bc", (128, E1))
    sgT_d = p.dram_in("sgT", (C, SHI))
    suT_d = p.dram_in("suT", (C, SHI))
    sdT_d = p.dram_in("sdT", (SHI, C))
    gwT_d = p.dram_in("gwT", (E1, C, I1))
    uwT_d = p.dram_in("uwT", (E1, C, I1))
    dwT_d = p.dram_in("dwT", (E1, I1, C))
    out_final = p.dram_out("out", (TPC, C))
    x5_d = p.dram_tmp("x5", (TPC, C))
    yacc2_d = p.dram_tmp("yacc2", (TPC, C))

    TOPm = [229344]

    def man(name, shape, dtype):
        esz = 2 if dtype == BF16 else 4
        TOPm[0] -= -(-shape[-1] * esz // 512) * 512
        return p.sb_at(name, list(shape), dtype, TOPm[0])

    xnt = [man(f"xnt{cc}", (128, TPC), BF16) for cc in range(NCC)]
    bufs = make_wkv_bufs(p, False, man)
    WKV_BASE = TOPm[0]
    wA = p.sb_at("l1_wA", (128, NCC * SHI), BF16, WKV_BASE)
    wB = p.sb_at("l1_wB", (128, NCC * SHI), BF16, WKV_BASE + 16384)
    wC = p.sb_at("l1_wC", (128, NCC * C), BF16, WKV_BASE + 32768)
    hTs = p.sb_at("l1_hTs", (128, 8 * 512), BF16, WKV_BASE + 49152)
    hTr = p.sb_at("l1_hTr", (128, 4 * 512), BF16, WKV_BASE + 57344)
    consts = setup_consts(p, _CONST_F)
    bufs["consts"] = consts
    st = LNState(p)
    st.bind_consts(consts)
    p.lnzero = st.zero
    stage = p.sb("ff_stage", (128, 512), F32)
    gsb = p.sb("l1_gsb", (128, 512), BF16)
    rw_sb = [p.sb(f"rw{cc}", (128, E1), BF16) for cc in range(NCC)]
    rbbc = p.sb("rbbc", (128, E1), F32)
    WROUT = p.sb("wrout", (128, NTT * E1), F32)
    scr = dict(psR=p.ps("rt_ps", (128, E1), F32),
               scores=p.sb("rt_sc", (128, E1), F32),
               sfc=p.sb("rt_sfc", (128, E1), F32),
               eqm=p.sb("rt_eqm", (128, E1), F32),
               tmp=p.sb("rt_tmp", (128, E1), F32),
               sel=p.sb("rt_sel", (128, E1), F32),
               w0=p.sb("rt_w0", (128, E1), F32),
               m1=p.sb("rt_m1", (128, 4), F32),
               m2=p.sb("rt_m2", (128, 4), F32),
               gsum=p.sb("rt_gs", (128, 4), F32),
               g1=p.sb("rt_g1", (128, 1), F32),
               g2=p.sb("rt_g2", (128, 1), F32),
               f1=p.sb("rt_f1", (128, 4), F32),
               gmask=p.sb("rt_gm", (128, 4), F32),
               mcol=p.sb("rt_mc", (128, 1), F32),
               wsum=p.sb("rt_ws", (128, 1), F32),
               winv=p.sb("rt_wi", (128, 1), F32))

    def ld0(eng):
        p.dma(eng, bufs["finsel"][:], finsel_d)
        p.dma(eng, bufs["cpow"][:], cpow_d)
        p.dma(eng, rbbc[:], rbbc_d)
    p.step("sync", ld0)
    _load_wkv_vectors(p, bufs, tm5_d, bounds_d)
    emit_ln_all(p, st, x4, xnt, consts)
    emit_wkv(p, xnt, False,
             dict(wk=wkT_d, wv=wvT_d, wr=wrT_d, wo=woT_d), bufs, consts,
             x4, x5_d)

    emit_ln_all(p, st, x5_d, xnt, consts)

    def ldrw(g):
        for cc in range(NCC):
            p.dma(g, rw_sb[cc][:], rwT_d[128 * cc:128 * (cc + 1), :])
    p.step("gpsimd", ldrw)
    emit_l1route(p, xnt, rw_sb, rbbc, WROUT, scr)
    emit_l1ffn(p, xnt, (sgT_d, suT_d, sdT_d, gwT_d, uwT_d, dwT_d), WROUT,
               yacc2_d, dict(wA=wA, wB=wB, wC=wC, hTs=hTs, hTr=hTr,
                             stage=stage, gsb=gsb))
    xs = p.sb_at("fin_x", (128, 1024), F32, WKV_BASE + 61440)
    ys = p.sb_at("fin_y", (128, 1024), F32, WKV_BASE + 65536)
    os_ = p.sb_at("fin_o", (128, 1024), F32, WKV_BASE + 69632)
    for i in range(NTT):
        def ld(eng, i=i):
            p.dma(eng, xs[:], x5_d[128 * i:128 * (i + 1), :])
            p.dma(eng, ys[:], yacc2_d[128 * i:128 * (i + 1), :])
        p.step("sync", ld)
        p.step("vector", lambda v: v.scalar_tensor_tensor(
            os_[:], ys[:], 0.5, xs[:], ALU.mult, ALU.add))
        p.step("sync", lambda eng, i=i: p.dma(
            eng, out_final[128 * i:128 * (i + 1), :], os_[:]))
    return p.build()


def host_wkv_vec_inputs(inputs, bounds_r):
    tm5 = np.empty((128, 40), np.float32)
    for i, nm in enumerate(["time_mix_k", "time_mix_v", "time_mix_r",
                            "time_decay", "time_first"]):
        tm5[:, 8 * i:8 * (i + 1)] = inputs[f"L1_attn_{nm}"].reshape(8, 128).T
    bq = np.zeros((128, 16), bfd)
    for bb in range(2):
        bq[:, 8 * bb:8 * (bb + 1)] = bounds_r[bb].reshape(8, 128).T
    return {"tm5": tm5, "bounds": bq,
            "wkT": np.ascontiguousarray(inputs["L1_attn_Wk"].T),
            "wvT": np.ascontiguousarray(inputs["L1_attn_Wv"].T)}


def host_bounds_for_core(bounds_shards, r):
    b = r // 4
    out = np.zeros((2, C), bfd)
    for bi, j in enumerate(blocks_of(r)):
        if j == 0:
            continue
        jprev = j - 1
        owner = 4 * b + min(jprev, 7 - jprev)
        bl = 0 if jprev == blocks_of(owner)[0] else 1
        out[bi] = bounds_shards[owner][bl]
    return out


def run_launch_E(x4_shards, bounds_shards, inputs):
    pE = build_launch_E()
    in_maps = []
    for r in range(NCORE):
        m = {"x4": x4_shards[r]}
        m.update(host_wkv_vec_inputs(inputs, host_bounds_for_core(bounds_shards, r)))
        m.update(const_inmap(_CONST_E))
        in_maps.append(m)
    res = run_prog(pE, in_maps)
    return [res[r]["finals"] for r in range(NCORE)]


def host_chain_inputs(finals_shards, r):
    b = r // 4
    finsel = np.zeros((128, 256), np.float32)
    cpow = np.zeros((128, 128), np.float32)
    for bi, jb in enumerate(blocks_of(r)):
        for cc in range(NCC):
            for j in range(8):
                if j >= jb:
                    continue
                owner = 4 * b + min(j, 7 - j)
                bl = 0 if j == blocks_of(owner)[0] else 1
                fin = finals_shards[owner]
                for ab in range(2):
                    finsel[:, (bi * 8 + cc) * 16 + ab * 8 + j] = \
                        fin[:, cc * 4 + bl * 2 + ab]
                cpow[:, (bi * 8 + cc) * 8 + j] = 1024.0 * (jb - 1 - j)
    return finsel, cpow


def run_launch_F(x4_shards, bounds_shards, finals_shards, inputs):
    pF = build_launch_F()
    tw = dict(
        wrT=np.ascontiguousarray(inputs["L1_attn_Wr"].T),
        woT=np.ascontiguousarray(inputs["L1_attn_Wo"].T),
        rwT=np.ascontiguousarray(inputs["L1_ffn_router_weight"].T),
        rb_bc=np.ascontiguousarray(
            np.broadcast_to(inputs["L1_ffn_router_bias"], (128, E1))),
        sgT=np.ascontiguousarray(inputs["L1_ffn_shared_gate_w"].T),
        suT=np.ascontiguousarray(inputs["L1_ffn_shared_up_w"].T),
        sdT=np.ascontiguousarray(inputs["L1_ffn_shared_down_w"].T),
        gwT=np.ascontiguousarray(inputs["L1_ffn_gate_w"].transpose(0, 2, 1)),
        uwT=np.ascontiguousarray(inputs["L1_ffn_up_w"].transpose(0, 2, 1)),
        dwT=np.ascontiguousarray(inputs["L1_ffn_down_w"].transpose(0, 2, 1)))
    in_maps = []
    for r in range(NCORE):
        finsel, cpow = host_chain_inputs(finals_shards, r)
        m = {"x4": x4_shards[r], "finsel": finsel, "cpow": cpow}
        m.update(host_wkv_vec_inputs(inputs, host_bounds_for_core(bounds_shards, r)))
        m.update(tw)
        m.update(const_inmap(_CONST_F))
        in_maps.append(m)
    res = run_prog(pF, in_maps)
    return [res[r]["out"] for r in range(NCORE)]


def host_probs_be(probsT_shards):
    pbe = np.empty((2, E0, 8, BLK), np.float32)
    for r in range(NCORE):
        b = r // 4
        j0, j1 = blocks_of(r)
        for e in range(E0):
            pbe[b, e, j0] = probsT_shards[r][e, :BLK]
            pbe[b, e, j1] = probsT_shards[r][e, BLK:]
    return pbe


def host_bisect_inputs(pbe, r):
    b = r // 4
    j0, j1 = blocks_of(r)
    gord = [j0, j1] + [j for j in range(8) if j not in (j0, j1)]
    batches = [b, 1 - b]
    pl = np.empty((128, 1024), np.float32)
    for bl in range(2):
        for e in range(E0):
            for pos in range(8):
                pl[bl * 64 + e * 8 + pos] = pbe[batches[bl], e, gord[pos]]
    gpre = np.zeros((128, 128), np.float32)
    for pp in range(128):
        for q in range(128):
            if pp // 8 == q // 8 and gord[pp % 8] < gord[q % 8]:
                gpre[pp, q] = 1.0
    return pl, gpre


def run_launch_B(x_shards, inputs, probsT_shards):
    pbe = host_probs_be(probsT_shards)
    p = build_launch_B()
    in_maps = []
    for r in range(NCORE):
        pl, gpre = host_bisect_inputs(pbe, r)
        m = {"x": x_shards[r], "probsL": pl, "gpre": gpre,
             "w1": inputs["L0_ffn_w1"], "w2": inputs["L0_ffn_w2"],
             "wkT": np.ascontiguousarray(inputs["L0_attn_Wk"].T),
             "wvT": np.ascontiguousarray(inputs["L0_attn_Wv"].T)}
        m.update(const_inmap(_CONST_B))
        in_maps.append(m)
    res = run_prog(p, in_maps)
    return ([res[r]["x1"] for r in range(NCORE)],
            [res[r]["K"] for r in range(NCORE)],
            [res[r]["V"] for r in range(NCORE)])


def kernel(**inputs):
    inputs = {k: np.asarray(v) for k, v in inputs.items()}
    x0 = np.ascontiguousarray(inputs["L0_attn_x"].reshape(B * T, C))
    x_shards = shard_tokens(x0)
    probsT = run_launch_A(x_shards, inputs)
    x1s, Ks, Vs = run_launch_B(x_shards, inputs, probsT)
    x2s, probsT2 = run_launch_C(x1s, Ks, Vs, inputs)
    x4s, bounds = run_launch_D(x2s, inputs, probsT2)
    finals = run_launch_E(x4s, bounds, inputs)
    outs = run_launch_F(x4s, bounds, finals, inputs)
    return unshard_tokens(outs).reshape(B, T, C).astype(np.float32)
